# version 01
# baseline (speedup 1.0000x reference)
import numpy as np
import concourse.bass as bass
import concourse.mybir as mybir
from concourse.bass_utils import run_bass_kernel_spmd
from contextlib import ExitStack

F32 = mybir.dt.float32
BF16 = mybir.dt.bfloat16
ALU = mybir.AluOpType
ACTF = mybir.ActivationFunctionType
AX = mybir.AxisListType

D = 2048
DC = 16
NT = 2048
TILE = 1024
H = 32
HD = 64
CW = 1056
NEG = -1.0e30
EPS = 1e-5
SCALE = HD ** -0.5
WIN = 8

ENGINES = ("sync", "tensor", "vector", "scalar", "gpsimd")


class Em:
    def __init__(self, nc, es):
        self.nc = nc
        self.ops = {e: [] for e in ENGINES}
        self.sems = {}
        self.counts = {}
        self.last_wait = {e: {} for e in ENGINES}
        self._vec_dirty = False
        for name in ("t", "v", "a", "g", "wd", "xd", "od", "gd"):
            self.sems[name] = es.enter_context(nc.semaphore("sem_" + name))
            self.counts[name] = 0

    def wait(self, eng, ev):
        if ev is None:
            return
        s, val = ev
        if val <= 0:
            return
        if self.last_wait[eng].get(s, 0) >= val:
            return
        self.last_wait[eng][s] = val
        h = self.sems[s]
        self.ops[eng].append(lambda e, h=h, val=val: e.wait_ge(h, val))

    def waits(self, eng, evs):
        for ev in evs:
            self.wait(eng, ev)

    def op(self, eng, fn, inc=None, dma=False, dep=True):
        if eng == "vector":
            if dep and self._vec_dirty:
                self.ops[eng].append(lambda e: e.drain())
            self._vec_dirty = True
        if inc is None:
            self.ops[eng].append(fn)
            return None
        n = 16 if dma else 1
        self.counts[inc] += n
        val = self.counts[inc]
        h = self.sems[inc]
        if not dma and eng in ("vector", "scalar"):
            self.ops[eng].append(fn)
            self.ops[eng].append(lambda e, h=h, n=n: e.drain().then_inc(h, n))
            if eng == "vector":
                self._vec_dirty = False
        else:
            self.ops[eng].append(lambda e, fn=fn, h=h, n=n: fn(e).then_inc(h, n))
        return (inc, val)


class Banks:
    def __init__(self, em):
        self.em = em
        self.w = {}
        self.r = {}

    def write(self, eng, bank):
        for ev in self.r.get(bank, {}).values():
            self.em.wait(eng, ev)
        self.em.wait(eng, self.w.get(bank))

    def wrote(self, bank, ev):
        self.w[bank] = ev
        self.r[bank] = {}

    def read(self, eng, bank):
        self.em.wait(eng, self.w.get(bank))

    def did_read(self, bank, eng, ev):
        self.r.setdefault(bank, {})[eng] = ev


def build_nc(steps=None, dbg=()):
    nc = bass.Bass("TRN2", target_bir_lowering=False)
    es = ExitStack()

    x_d = nc.dram_tensor("x", [NT, D], F32, kind="ExternalInput").ap()
    wqk_d = nc.dram_tensor("wqk", [32, 128, DC * 128], F32, kind="ExternalInput").ap()
    wv_d = nc.dram_tensor("wv", [4, 128, DC * 512], F32, kind="ExternalInput").ap()
    wp_d = nc.dram_tensor("wp", [4, 128, DC * 512], F32, kind="ExternalInput").ap()
    w1_d = nc.dram_tensor("w1", [16, 128, DC * 128], F32, kind="ExternalInput").ap()
    w3_d = nc.dram_tensor("w3", [16, 128, DC * 128], F32, kind="ExternalInput").ap()
    w2_d = nc.dram_tensor("w2", [4, 128, DC * 512], F32, kind="ExternalInput").ap()
    wr_d = nc.dram_tensor("wr", [128, DC * 8], F32, kind="ExternalInput").ap()
    pb_d = nc.dram_tensor("pb", [1, D], F32, kind="ExternalInput").ap()
    mmm_d = nc.dram_tensor("mm_main", [128, 144], F32, kind="ExternalInput").ap()
    amm_d = nc.dram_tensor("am_main", [128, 144], F32, kind="ExternalInput").ap()
    mms_d = nc.dram_tensor("mm_s0", [8, 16], F32, kind="ExternalInput").ap()
    ams_d = nc.dram_tensor("am_s0", [8, 16], F32, kind="ExternalInput").ap()
    mml_d = nc.dram_tensor("mm_last", [120, 128], F32, kind="ExternalInput").ap()
    aml_d = nc.dram_tensor("am_last", [120, 128], F32, kind="ExternalInput").ap()
    id_d = nc.dram_tensor("ident", [128, 128], F32, kind="ExternalInput").ap()
    cm_d = nc.dram_tensor("coremask", [8, 1], F32, kind="ExternalInput").ap()
    y_d = nc.dram_tensor("y", [NT, D], F32, kind="ExternalOutput").ap()
    cc_i = nc.dram_tensor("cc_i", [8, D], F32)
    cc_o = nc.dram_tensor("cc_o", [8, D], F32)
    dbg_d = {}

    sb = lambda name, shp, dt: es.enter_context(nc.sbuf_tensor(name, shp, dt))
    XNT = sb("XNT", [128, DC * CW], BF16)
    VB = sb("VB", [128, 9 * 2048], BF16)
    ATT = sb("ATT", [128, DC * 1024], BF16)
    SLV = sb("SLV", [128, DC * 8], BF16)
    QKB = sb("QKB", [128, 4 * CW], BF16)
    WA = sb("WA", [128, 2 * 2048], BF16)
    WB = sb("WB", [128, 2 * DC * 512], BF16)
    XCH = sb("XCH", [128, 2 * 2048], F32)
    XBK = sb("XBK", [128, 2 * 128], BF16)
    XBK2 = sb("XBK2", [128, 2 * 128], F32)
    F32R = sb("F32R", [128, 2 * 128], F32)
    XSL = sb("XSL", [128, 2 * 512], F32)
    XPS = sb("XPS", [128, 2 * 512], F32)
    T1R = sb("T1R", [128, 2 * 144], F32)
    PR = sb("PR", [128, 2 * 160], BF16)
    PS0R = sb("PS0R", [32, 32], BF16)
    PTM = sb("PTM", [128, 2 * 128], BF16)
    PTH = sb("PTH", [32, 2 * 128], BF16)
    STT = sb("STT", [128, 32], F32)
    SST = sb("SST", [128, 16], F32)
    RPS = sb("RPS", [128, 40], F32)
    GT = sb("GT", [128, 8 * 8], BF16)
    MMM = sb("MMM", [128, 144], F32)
    AMM = sb("AMM", [128, 144], F32)
    MMS = sb("MMS", [8, 16], F32)
    AMS = sb("AMS", [8, 16], F32)
    MML = sb("MML", [120, 128], F32)
    AML = sb("AML", [120, 128], F32)
    IDB = sb("IDB", [128, 128], BF16)
    IDF = sb("IDF", [128, 128], F32)
    ONE = sb("ONE", [1, 128], BF16)
    PBB = sb("PBB", [1, D], BF16)
    WR = sb("WR", [128, DC * 8], F32)
    CM = sb("CM", [8, 1], F32)
    ZAP = sb("ZAP", [128, 1], F32)
    EPSAP = sb("EPSAP", [128, 1], F32)
    SQSL = sb("SQSL", [128, 512], F32)
    FIN = sb("FIN", [8, 2 * D], F32)

    ps = lambda name, shp, dt: es.enter_context(nc.psum_tensor(name, shp, dt))
    PM = [ps("PM0", [128, 512], F32), ps("PM1", [128, 512], F32)]
    PS = [ps("PS0", [128, 512], F32), ps("PS1", [128, 512], F32)]
    PV = [ps("PV0", [128, 512], F32), ps("PV1", [128, 512], F32)]
    PT = [ps("PT0", [128, 1024], BF16), ps("PT1", [128, 1024], BF16)]
    BK_PM, BK_PS, BK_PV, BK_PT = (0, 1), (2, 3), (4, 5), (6, 7)

    em = Em(nc, es)
    bk = Banks(em)

    ld_evs = []
    for dst, src in ((MMM, mmm_d), (AMM, amm_d), (MMS, mms_d), (AMS, ams_d),
                     (MML, mml_d), (AML, aml_d), (IDF, id_d), (WR, wr_d),
                     (CM, cm_d)):
        ld_evs.append(em.op("sync", lambda e, d=dst, s=src: e.dma_start(
            out=d[:, :], in_=s[:, :]), inc="xd", dma=True))
    ld_evs.append(em.op("gpsimd", lambda e: e.dma_start(
        out=IDB[:, :], in_=id_d[:, :]), inc="wd", dma=True))
    ld_evs.append(em.op("gpsimd", lambda e: e.dma_start(
        out=PBB[:, :], in_=pb_d[:, :]), inc="wd", dma=True))
    em.waits("vector", ld_evs)
    em.op("vector", lambda e: e.memset(ZAP[:, :], 0.0))
    em.op("vector", lambda e: e.memset(EPSAP[:, :], EPS))
    em.op("vector", lambda e: e.memset(PR[:, 144:160], 0.0))
    em.op("vector", lambda e: e.memset(PR[:, 304:320], 0.0))
    em.op("vector", lambda e: e.memset(PS0R[:, :], 0.0))
    em.op("vector", lambda e: e.memset(VB[0:32, 8 * 2048:9 * 2048], 0.0))
    ev_memset = em.op("vector", lambda e: e.memset(ONE[:, :], 1.0), inc="v")
    em.waits("tensor", ld_evs + [ev_memset])
    em.waits("scalar", ld_evs + [ev_memset])

    state = {
        "wa": [None, None], "wb": [None, None],
        "wa_cons": [None, None], "wb_cons": [None, None],
        "xch_cons": [None, None],
        "qk_cons": [None, None, None, None],
        "xsl_cons": [None, None], "xps_cons": [None, None],
        "t1_cons": [None, None], "pr_cons": [None, None],
        "ptm_cons": [None, None], "pth_cons": [None, None],
        "xbk_cons": [None, None], "xbk2_cons": [None, None],
        "f32r_cons": [None, None],
        "y_written": {},
        "ln_ev": {"vector": None, "scalar": None},
        "v_done": {"vector": None, "scalar": None},
    }

    def load_wa(slot, src_ap):
        em.wait("gpsimd", state["wa_cons"][slot])
        ev = em.op("gpsimd", lambda e, s=src_ap, sl=slot: e.dma_start(
            out=WA[:, sl * 2048:(sl + 1) * 2048], in_=s,
            max_dma_last_dim=8192), inc="wd", dma=True)
        state["wa"][slot] = ev
        return ev

    def load_wb(slot, src_ap):
        em.wait("gpsimd", state["wb_cons"][slot])
        ev = em.op("gpsimd", lambda e, s=src_ap, sl=slot: e.dma_start(
            out=WB[:, sl * DC * 512:(sl + 1) * DC * 512], in_=s,
            max_dma_last_dim=8192), inc="wd", dma=True)
        state["wb"][slot] = ev
        return ev

    def ln_chunk(tile, i, nrow, rows_src, is_ln2):
        sl = i % 2
        em.wait("sync", state["xch_cons"][sl])
        if is_ln2:
            em.wait("sync", state["y_written"][rows_src // 128])
            src = y_d[rows_src:rows_src + nrow, :]
        else:
            src = x_d[rows_src:rows_src + nrow, :]
        ev_x = em.op("sync", lambda e, s=src, sl=sl, n=nrow: e.dma_start(
            out=XCH[0:n, sl * 2048: sl * 2048 + 2048], in_=s), inc="xd", dma=True)
        em.wait("vector", ev_x)
        em.op("vector", lambda e, sl=sl, n=nrow: e.reduce_sum(
            STT[0:n, 23:24], XCH[0:n, sl * 2048: sl * 2048 + 2048], axis=AX.X))
        em.op("vector", lambda e, n=nrow: e.tensor_scalar(
            out=STT[0:n, 24:25], in0=STT[0:n, 23:24], scalar1=1.0 / D,
            scalar2=None, op0=ALU.mult))
        for q in range(4):
            em.op("vector", lambda e, q=q, sl=sl, n=nrow: e.scalar_tensor_tensor(
                out=SQSL[0:n, 0:512],
                in0=XCH[0:n, sl * 2048 + q * 512: sl * 2048 + (q + 1) * 512],
                scalar=1.0,
                in1=XCH[0:n, sl * 2048 + q * 512: sl * 2048 + (q + 1) * 512],
                op0=ALU.mult, op1=ALU.mult))
            em.op("vector", lambda e, q=q, n=nrow: e.reduce_sum(
                STT[0:n, 16 + q:17 + q], SQSL[0:n, 0:512], axis=AX.X))
        em.op("vector", lambda e, n=nrow: e.tensor_tensor(
            out=STT[0:n, 20:21], in0=STT[0:n, 16:17], in1=STT[0:n, 17:18], op=ALU.add))
        em.op("vector", lambda e, n=nrow: e.tensor_tensor(
            out=STT[0:n, 21:22], in0=STT[0:n, 18:19], in1=STT[0:n, 19:20], op=ALU.add))
        em.op("vector", lambda e, n=nrow: e.tensor_tensor(
            out=STT[0:n, 22:23], in0=STT[0:n, 20:21], in1=STT[0:n, 21:22], op=ALU.add))
        em.op("vector", lambda e, n=nrow: e.tensor_scalar(
            out=STT[0:n, 26:27], in0=STT[0:n, 22:23], scalar1=1.0 / D,
            scalar2=None, op0=ALU.mult))
        em.op("vector", lambda e, n=nrow: e.tensor_tensor(
            out=STT[0:n, 18:19], in0=STT[0:n, 24:25], in1=STT[0:n, 24:25], op=ALU.mult))
        ev_mv = em.op("vector", lambda e, n=nrow: e.tensor_tensor(
            out=STT[0:n, 25:26], in0=STT[0:n, 26:27], in1=STT[0:n, 18:19],
            op=ALU.subtract), inc="v")
        em.wait("scalar", ev_mv)
        ev_sd = em.op("scalar", lambda e, n=nrow: e.activation(
            out=STT[0:n, 27:28], in_=STT[0:n, 25:26], func=ACTF.Sqrt,
            bias=EPSAP[0:n, 0:1], scale=1.0), inc="a")
        em.wait("vector", ev_sd)
        em.op("vector", lambda e, n=nrow: e.reciprocal(
            STT[0:n, 28:29], STT[0:n, 27:28]))
        ev_nb = em.op("vector", lambda e, n=nrow: e.tensor_scalar(
            out=STT[0:n, 29:30], in0=STT[0:n, 24:25], scalar1=STT[0:n, 28:29],
            scalar2=-1.0, op0=ALU.mult, op1=ALU.mult), inc="v")
        em.wait("scalar", ev_nb)

        t0 = i * 128
        ev_nm = None
        for c in range(DC):
            bs = c % 2
            dst = XBK2 if is_ln2 else XBK
            em.wait("scalar", state["xbk2_cons" if is_ln2 else "xbk_cons"][bs])
            ev_nm = em.op("scalar", lambda e, c=c, sl=sl, bs=bs, n=nrow, d=dst: e.activation(
                out=d[0:n, bs * 128:bs * 128 + 128],
                in_=XCH[0:n, sl * 2048 + c * 128: sl * 2048 + (c + 1) * 128],
                func=ACTF.Identity, bias=STT[0:n, 29:30], scale=STT[0:n, 28:29]),
                inc="a")
            if is_ln2:
                pmb = BK_PM[bs]
                bk.write("tensor", pmb)
                em.wait("tensor", ev_nm)
                ev_tr = em.op("tensor", lambda e, bs=bs, n=nrow: e.transpose(
                    PM[bs][0:128, 0:n], XBK2[0:n, bs * 128:bs * 128 + 128],
                    IDF[0:n, 0:n]), inc="t")
                bk.wrote(pmb, ev_tr)
                state["xbk2_cons"][bs] = ev_tr
                em.wait("vector", ev_tr)
                em.wait("vector", state["f32r_cons"][bs])
                ev_f = em.op("vector", lambda e, bs=bs, n=nrow: e.tensor_copy(
                    F32R[:, bs * 128:bs * 128 + n], PM[bs][0:128, 0:n]), inc="v")
                bk.did_read(pmb, "vector", ev_f)
                em.wait("scalar", ev_f)
                ev_cb = em.op("scalar", lambda e, bs=bs, c=c, t0=t0, n=nrow: e.activation(
                    out=XNT[:, c * CW + t0: c * CW + t0 + n],
                    in_=PM[bs][0:128, 0:n], func=ACTF.Copy, bias=0.0, scale=1.0),
                    inc="a")
                bk.did_read(pmb, "scalar", ev_cb)
                state["ln_ev"]["scalar"] = ev_cb
                bk.write("tensor", BK_PS[0])
                em.wait("tensor", ev_f)
                ev_rmm = em.op("tensor", lambda e, bs=bs, c=c, i=i, n=nrow: e.matmul(
                    PS[0][0:n, 8 * i: 8 * i + 8],
                    lhsT=F32R[:, bs * 128:bs * 128 + n],
                    rhs=WR[:, 8 * c:8 * c + 8],
                    start=(c == 0), stop=(c == DC - 1)), inc="t")
                state["f32r_cons"][bs] = ev_rmm
                if c == DC - 1:
                    bk.wrote(BK_PS[0], ev_rmm)
            else:
                ptb = BK_PT[bs]
                bk.write("tensor", ptb)
                em.wait("tensor", ev_nm)
                ev_tr = em.op("tensor", lambda e, bs=bs, n=nrow: e.transpose(
                    PT[bs][0:128, 0:n], XBK[0:n, bs * 128:bs * 128 + 128],
                    IDB[0:n, 0:n]), inc="t")
                bk.wrote(ptb, ev_tr)
                state["xbk_cons"][bs] = ev_tr
                eng = "vector" if c % 2 == 0 else "scalar"
                em.wait(eng, ev_tr)
                if eng == "vector":
                    ev_e = em.op("vector", lambda e, bs=bs, c=c, t0=t0, n=nrow: e.tensor_copy(
                        XNT[:, c * CW + t0: c * CW + t0 + n], PT[bs][0:128, 0:n]),
                        inc="v")
                else:
                    ev_e = em.op("scalar", lambda e, bs=bs, c=c, t0=t0, n=nrow: e.activation(
                        out=XNT[:, c * CW + t0: c * CW + t0 + n],
                        in_=PT[bs][0:128, 0:n], func=ACTF.Copy, bias=0.0, scale=1.0),
                        inc="a")
                bk.did_read(ptb, eng, ev_e)
                state["ln_ev"][eng] = ev_e
        state["xch_cons"][sl] = ev_nm

    def ln_phase(tile, is_ln2):
        base = tile * TILE
        ntok = 1040 if (tile == 0 and not is_ln2) else 1024
        nch = (ntok + 127) // 128
        for i in range(nch):
            nrow = min(128, ntok - i * 128)
            ln_chunk(tile, i, nrow, base + i * 128, is_ln2)
        if is_ln2:
            router_post(tile)

    def router_post(tile):
        RL = T1R
        bk.read("vector", BK_PS[0])
        em.wait("vector", state["t1_cons"][0])
        em.wait("vector", state["t1_cons"][1])
        ev_rl = em.op("vector", lambda e: e.tensor_copy(
            RL[:, 0:128], PS[0][0:128, 0:128]), inc="v")
        bk.did_read(BK_PS[0], "vector", ev_rl)
        ev_g = None
        for i in range(8):
            em.op("vector", lambda e, i=i: e.reduce_max(
                RPS[:, 32:33], RL[:, 8 * i:8 * i + 8], axis=AX.X, negate=True))
            ev1 = em.op("vector", lambda e, i=i: e.tensor_scalar(
                out=RPS[:, 0:8], in0=RL[:, 8 * i:8 * i + 8], scalar1=RPS[:, 32:33],
                scalar2=None, op0=ALU.add), inc="v")
            em.wait("scalar", ev1)
            ev2 = em.op("scalar", lambda e: e.activation(
                out=RPS[:, 0:8], in_=RPS[:, 0:8], func=ACTF.Exp, bias=ZAP[:, 0:1],
                scale=1.0), inc="a")
            em.wait("vector", ev2)
            em.op("vector", lambda e: e.reduce_sum(
                RPS[:, 33:34], RPS[:, 0:8], axis=AX.X))
            em.op("vector", lambda e: e.reciprocal(RPS[:, 34:35], RPS[:, 33:34]))
            em.op("vector", lambda e: e.tensor_scalar(
                out=RPS[:, 0:8], in0=RPS[:, 0:8], scalar1=RPS[:, 34:35],
                scalar2=None, op0=ALU.mult))
            em.op("vector", lambda e: e.max(RPS[:, 8:16], RPS[:, 0:8]))
            em.op("vector", lambda e: e.tensor_scalar(
                out=RPS[:, 16:24], in0=RPS[:, 0:8], scalar1=RPS[:, 9:10],
                scalar2=None, op0=ALU.is_ge))
            em.op("vector", lambda e: e.tensor_tensor(
                out=RPS[:, 35:36], in0=RPS[:, 8:9], in1=RPS[:, 9:10], op=ALU.add))
            em.op("vector", lambda e: e.reciprocal(RPS[:, 36:37], RPS[:, 35:36]))
            ev_g = em.op("vector", lambda e, i=i: e.scalar_tensor_tensor(
                out=GT[:, 8 * i:8 * i + 8], in0=RPS[:, 16:24], scalar=RPS[:, 36:37],
                in1=RPS[:, 0:8], op0=ALU.mult, op1=ALU.mult), inc="v")
        state["gates_ev"] = ev_g
        state["rl_ev"] = ev_rl

    def v_phase(tile):
        ntok = 1040 if tile == 0 else 1024
        nj = 9 if tile == 0 else 8
        em.waits("tensor", [state["ln_ev"]["vector"], state["ln_ev"]["scalar"]])
        ev_e = None
        for n in range(4):
            sl = n % 2
            load_wb(sl, wv_d[n])
            ev_w = state["wb"][sl]
            for j in range(nj):
                nrow = min(128, ntok - j * 128)
                pm = (n * 9 + j) % 2
                pmb = BK_PM[pm]
                bk.write("tensor", pmb)
                em.wait("tensor", ev_w)
                ev_mm = None
                for c in range(DC):
                    ev_mm = em.op("tensor", lambda e, c=c, j=j, sl=sl, pm=pm, nr=nrow: e.matmul(
                        PM[pm][0:nr, 0:512],
                        lhsT=XNT[:, c * CW + j * 128: c * CW + j * 128 + nr],
                        rhs=WB[:, sl * DC * 512 + c * 512: sl * DC * 512 + (c + 1) * 512],
                        start=(c == 0), stop=(c == DC - 1)),
                        inc=("t" if c == DC - 1 else None))
                bk.wrote(pmb, ev_mm)
                state["wb_cons"][sl] = ev_mm
                eng = "vector" if j % 2 == 0 else "scalar"
                em.wait(eng, ev_mm)
                if eng == "vector":
                    ev_e = em.op("vector", lambda e, j=j, n=n, pm=pm, nr=nrow: e.tensor_copy(
                        VB[0:nr, j * 2048 + n * 512: j * 2048 + (n + 1) * 512],
                        PM[pm][0:nr, 0:512]), inc="v")
                else:
                    ev_e = em.op("scalar", lambda e, j=j, n=n, pm=pm, nr=nrow: e.activation(
                        out=VB[0:nr, j * 2048 + n * 512: j * 2048 + (n + 1) * 512],
                        in_=PM[pm][0:nr, 0:512], func=ACTF.Copy, bias=0.0,
                        scale=1.0), inc="a")
                bk.did_read(pmb, eng, ev_e)
                state["v_done"][eng] = ev_e

    def units_for_tile(tile):
        if tile == 0:
            us = [("s0", 0, 8, 0, 16, 0)]
            us += [("main", 8 + 128 * j, 128, 128 * j, 144, j) for j in range(8)]
        else:
            us = [("main", 8 + 128 * j, 128, 128 * j, 144, j) for j in range(7)]
            us.append(("last", 904, 120, 896, 128, 7))
        return us

    def qk_mchunk(tile, pair, kind, ncols):
        sl = pair % 2
        ring = kind * 2 + sl
        blk = 2 * pair + kind
        wsl = blk % 2
        load_wa(wsl, wqk_d[blk])
        ev_w = state["wa"][wsl]
        nsl = [(0, 512), (512, 512), (1024, 16)] if ncols == 1040 else [(0, 512), (512, 512)]
        ev_e = None
        for (off, n) in nsl:
            state["qk_pm"] = (state.get("qk_pm", 0) + 1) % 2
            pm = state["qk_pm"]
            pmb = BK_PM[pm]
            bk.write("tensor", pmb)
            em.wait("tensor", ev_w)
            ev_mm = None
            for c in range(DC):
                ev_mm = em.op("tensor", lambda e, c=c, off=off, n=n, wsl=wsl, pm=pm: e.matmul(
                    PM[pm][0:128, 0:n],
                    lhsT=WA[:, wsl * 2048 + c * 128: wsl * 2048 + (c + 1) * 128],
                    rhs=XNT[:, c * CW + off: c * CW + off + n],
                    start=(c == 0), stop=(c == DC - 1)),
                    inc=("t" if c == DC - 1 else None))
            bk.wrote(pmb, ev_mm)
            state["wa_cons"][wsl] = ev_mm
            eng = "vector" if kind == 0 else "scalar"
            em.wait(eng, ev_mm)
            em.wait(eng, state["qk_cons"][ring])
            if eng == "vector":
                ev_e = em.op("vector", lambda e, ring=ring, off=off, n=n, pm=pm: e.tensor_copy(
                    QKB[:, ring * CW + off: ring * CW + off + n], PM[pm][0:128, 0:n]),
                    inc="v")
            else:
                ev_e = em.op("scalar", lambda e, ring=ring, off=off, n=n, pm=pm: e.activation(
                    out=QKB[:, ring * CW + off: ring * CW + off + n],
                    in_=PM[pm][0:128, 0:n], func=ACTF.Copy, bias=0.0, scale=1.0),
                    inc="a")
            bk.did_read(pmb, eng, ev_e)
        state.setdefault("qk_ready", {})[(pair, kind)] = ev_e

    def attn_unit(tile, head, uidx, unit):
        kind, q0, nq, k0, nk, vj = unit
        par = head % 2
        pair = head // 2
        rsl = pair % 2
        p0 = 64 * par
        state["ucnt"] = state.get("ucnt", 0) + 1
        u = state["ucnt"] % 2
        sb0 = 4 * u
        qr = QKB[p0:p0 + 64, rsl * CW + q0: rsl * CW + q0 + nq]
        kr = QKB[p0:p0 + 64, (2 + rsl) * CW + k0: (2 + rsl) * CW + k0 + nk]
        ev_qk_q = state["qk_ready"][(pair, 0)]
        ev_qk_k = state["qk_ready"][(pair, 1)]
        psb = BK_PS[u]
        bk.write("tensor", psb)
        em.waits("tensor", [ev_qk_q, ev_qk_k])
        ev_sc = em.op("tensor", lambda e, qr=qr, kr=kr, u=u, nq=nq, nk=nk: e.matmul(
            PS[u][0:nq, 0:nk], lhsT=qr, rhs=kr, start=True, stop=True), inc="t")
        bk.wrote(psb, ev_sc)
        state["qk_cons"][rsl] = ev_sc
        state["qk_cons"][2 + rsl] = ev_sc
        mm, am = {"main": (MMM, AMM), "s0": (MMS, AMS), "last": (MML, AML)}[kind]
        em.wait("vector", ev_sc)
        em.wait("vector", state["t1_cons"][u])
        em.op("vector", lambda e, u=u, nq=nq, nk=nk, mm=mm: e.scalar_tensor_tensor(
            out=T1R[0:nq, u * 144: u * 144 + nk], in0=PS[u][0:nq, 0:nk],
            scalar=1.0, in1=mm[0:nq, 0:nk], op0=ALU.mult, op1=ALU.mult))
        ev_t1 = em.op("vector", lambda e, u=u, nq=nq, nk=nk, am=am: e.tensor_tensor(
            out=T1R[0:nq, u * 144: u * 144 + nk],
            in0=T1R[0:nq, u * 144: u * 144 + nk],
            in1=am[0:nq, 0:nk], op=ALU.add), inc="v")
        bk.did_read(psb, "vector", ev_t1)
        ev_nm = em.op("vector", lambda e, u=u, nq=nq, nk=nk, sb0=sb0: e.reduce_max(
            SST[0:nq, sb0:sb0 + 1], T1R[0:nq, u * 144: u * 144 + nk], axis=AX.X,
            negate=True), inc="v")
        em.waits("scalar", [ev_t1, ev_nm])
        ev_ex = em.op("scalar", lambda e, u=u, nq=nq, nk=nk, sb0=sb0: e.activation(
            out=T1R[0:nq, u * 144: u * 144 + nk],
            in_=T1R[0:nq, u * 144: u * 144 + nk], func=ACTF.Exp,
            bias=SST[0:nq, sb0:sb0 + 1], scale=1.0), inc="a")
        em.wait("vector", ev_ex)
        em.op("vector", lambda e, u=u, nq=nq, nk=nk, sb0=sb0: e.reduce_sum(
            SST[0:nq, sb0 + 1:sb0 + 2], T1R[0:nq, u * 144: u * 144 + nk], axis=AX.X))
        ev_rr = em.op("vector", lambda e, nq=nq, sb0=sb0: e.reciprocal(
            SST[0:nq, sb0 + 2:sb0 + 3], SST[0:nq, sb0 + 1:sb0 + 2]), inc="v")
        em.wait("scalar", ev_rr)
        em.wait("scalar", state["pr_cons"][u])
        if kind == "s0":
            ev_p = em.op("scalar", lambda e, u=u, sb0=sb0: e.activation(
                out=PS0R[0:8, 0:16], in_=T1R[0:8, u * 144: u * 144 + 16],
                func=ACTF.Identity, bias=ZAP[0:8, 0:1], scale=SST[0:8, sb0 + 2:sb0 + 3]),
                inc="a")
        else:
            ev_p = em.op("scalar", lambda e, u=u, nq=nq, nk=nk, sb0=sb0: e.activation(
                out=PR[0:nq, u * 160: u * 160 + nk],
                in_=T1R[0:nq, u * 144: u * 144 + nk],
                func=ACTF.Identity, bias=ZAP[0:nq, 0:1], scale=SST[0:nq, sb0 + 2:sb0 + 3]),
                inc="a")
        state["t1_cons"][u] = ev_p
        ptb = BK_PT[u]
        bk.write("tensor", ptb)
        em.wait("tensor", ev_p)
        if kind == "s0":
            ev_tr = em.op("tensor", lambda e, u=u: e.transpose(
                PT[u][0:32, 0:32], PS0R[0:32, 0:32], IDB[0:32, 0:32]), inc="t")
            state["pr_cons"][u] = ev_tr
            bk.wrote(ptb, ev_tr)
            em.wait("vector", ev_tr)
            em.wait("vector", state["pth_cons"][u])
            ev_pth = em.op("vector", lambda e, u=u: e.tensor_copy(
                PTH[0:32, u * 128: u * 128 + 32], PT[u][0:32, 0:32]), inc="v")
            bk.did_read(ptb, "vector", ev_pth)
            ev_ptm = ev_pth
        else:
            ev_tr = em.op("tensor", lambda e, u=u: e.transpose(
                PT[u][0:128, 0:128], PR[:, u * 160: u * 160 + 128], IDB[:, :]),
                inc="t")
            if kind == "main":
                ev_tr = em.op("tensor", lambda e, u=u: e.transpose(
                    PT[u][0:32, 128:256], PR[:, u * 160 + 128: u * 160 + 160],
                    IDB[:, :]), inc="t")
            state["pr_cons"][u] = ev_tr
            bk.wrote(ptb, ev_tr)
            em.wait("vector", ev_tr)
            em.wait("vector", state["ptm_cons"][u])
            ev_ptm = em.op("vector", lambda e, u=u: e.tensor_copy(
                PTM[:, u * 128: u * 128 + 128], PT[u][0:128, 0:128]), inc="v")
            ev_pth = ev_ptm
            if kind == "main":
                em.wait("vector", state["pth_cons"][u])
                ev_pth = em.op("vector", lambda e, u=u: e.tensor_copy(
                    PTH[0:32, u * 128: u * 128 + 128], PT[u][0:32, 128:256]),
                    inc="v")
            bk.did_read(ptb, "vector", ev_pth)
        pvb = BK_PV[u]
        bk.write("tensor", pvb)
        em.waits("tensor", [ev_ptm, ev_pth,
                            state["v_done"]["vector"], state["v_done"]["scalar"]])
        tp = (0, p0)
        dvc = head * 64
        ch = head // 2
        dv0 = 64 * par
        if kind == "s0":
            ev_pv = em.op("tensor", lambda e, u=u, p0=p0, dvc=dvc, tp=tp: e.matmul(
                PV[u][p0:p0 + 64, 0:8],
                lhsT=VB[0:32, dvc: dvc + 64],
                rhs=PTH[0:32, u * 128: u * 128 + 8],
                start=True, stop=True, tile_position=tp), inc="t")
        elif kind == "last":
            ev_pv = em.op("tensor", lambda e, u=u, p0=p0, dvc=dvc, vj=vj, nq=nq, tp=tp: e.matmul(
                PV[u][p0:p0 + 64, 0:nq],
                lhsT=VB[:, vj * 2048 + dvc: vj * 2048 + dvc + 64],
                rhs=PTM[:, u * 128: u * 128 + nq],
                start=True, stop=True, tile_position=tp), inc="t")
        else:
            em.op("tensor", lambda e, u=u, p0=p0, dvc=dvc, vj=vj, tp=tp: e.matmul(
                PV[u][p0:p0 + 64, 0:128],
                lhsT=VB[:, vj * 2048 + dvc: vj * 2048 + dvc + 64],
                rhs=PTM[:, u * 128: u * 128 + 128],
                start=True, stop=False, tile_position=tp))
            ev_pv = em.op("tensor", lambda e, u=u, p0=p0, dvc=dvc, vj=vj, tp=tp: e.matmul(
                PV[u][p0:p0 + 64, 0:128],
                lhsT=VB[0:32, (vj + 1) * 2048 + dvc: (vj + 1) * 2048 + dvc + 64],
                rhs=PTH[0:32, u * 128: u * 128 + 128],
                start=False, stop=True, tile_position=tp), inc="t")
        state["ptm_cons"][u] = ev_pv
        state["pth_cons"][u] = ev_pv
        bk.wrote(pvb, ev_pv)
        em.wait("vector", ev_pv)
        if tile == 0 and kind == "main" and vj == 7:
            em.op("vector", lambda e, u=u, p0=p0, dv0=dv0, ch=ch, q0=q0: e.tensor_copy(
                ATT[dv0:dv0 + 64, ch * 1024 + q0: ch * 1024 + 1024],
                PV[u][p0:p0 + 64, 0:120]))
            ev_av = em.op("vector", lambda e, u=u, p0=p0, dv0=dv0, ch=ch: e.tensor_copy(
                SLV[dv0:dv0 + 64, ch * 8: ch * 8 + 8],
                PV[u][p0:p0 + 64, 120:128]), inc="v")
        else:
            ev_av = em.op("vector", lambda e, u=u, p0=p0, dv0=dv0, ch=ch, q0=q0, nq=nq: e.tensor_copy(
                ATT[dv0:dv0 + 64, ch * 1024 + q0: ch * 1024 + q0 + nq],
                PV[u][p0:p0 + 64, 0:nq]), inc="v")
        bk.did_read(pvb, "vector", ev_av)
        state["attn_last_ev"] = ev_av

    def qk_attn_phase(tile):
        ncols = 1040 if tile == 0 else 1024
        units = units_for_tile(tile)
        if tile == 1:
            for c in range(DC):
                ev_sv = em.op("vector", lambda e, c=c: e.tensor_copy(
                    ATT[:, c * 1024: c * 1024 + 8], SLV[:, c * 8: c * 8 + 8]),
                    inc="v")
            state["sliver_ev"] = ev_sv
        qk_mchunk(tile, 0, 0, ncols)
        qk_mchunk(tile, 0, 1, ncols)
        for pair in range(16):
            if pair + 1 < 16:
                qk_mchunk(tile, pair + 1, 0, ncols)
                qk_mchunk(tile, pair + 1, 1, ncols)
            for hh in range(2):
                head = 2 * pair + hh
                for uidx, unit in enumerate(units):
                    attn_unit(tile, head, uidx, unit)

    def proj_phase(tile):
        base = tile * TILE
        em.wait("tensor", state.get("attn_last_ev"))
        if tile == 1:
            em.wait("tensor", state.get("sliver_ev"))
        for n in range(4):
            sl = n % 2
            load_wb(sl, wp_d[n])
            ev_w = state["wb"][sl]
            for t in range(8):
                pm = (n * 8 + t) % 2
                pmb = BK_PM[pm]
                bk.write("tensor", pmb)
                em.wait("tensor", ev_w)
                for c in range(DC):
                    em.op("tensor", lambda e, c=c, t=t, sl=sl, pm=pm: e.matmul(
                        PM[pm][0:128, 0:512],
                        lhsT=ATT[:, c * 1024 + t * 128: c * 1024 + (t + 1) * 128],
                        rhs=WB[:, sl * DC * 512 + c * 512: sl * DC * 512 + (c + 1) * 512],
                        start=(c == 0), stop=False))
                ev_mm = em.op("tensor", lambda e, n=n, pm=pm: e.matmul(
                    PM[pm][0:128, 0:512], lhsT=ONE[0:1, 0:128],
                    rhs=PBB[0:1, n * 512:(n + 1) * 512],
                    start=False, stop=True), inc="t")
                bk.wrote(pmb, ev_mm)
                state["wb_cons"][sl] = ev_mm
                xs = t % 2
                em.wait("sync", state["xsl_cons"][xs])
                ev_x = em.op("sync", lambda e, t=t, n=n, xs=xs, base=base: e.dma_start(
                    out=XSL[:, xs * 512:(xs + 1) * 512],
                    in_=x_d[base + t * 128: base + (t + 1) * 128, n * 512:(n + 1) * 512]),
                    inc="xd", dma=True)
                em.waits("vector", [ev_mm, ev_x])
                em.wait("vector", state["xps_cons"][xs])
                ev_xp = em.op("vector", lambda e, xs=xs, pm=pm: e.tensor_tensor(
                    out=XPS[:, xs * 512:(xs + 1) * 512],
                    in0=PM[pm][0:128, 0:512], in1=XSL[:, xs * 512:(xs + 1) * 512],
                    op=ALU.add), inc="v")
                bk.did_read(pmb, "vector", ev_xp)
                state["xsl_cons"][xs] = ev_xp
                em.wait("scalar", ev_xp)
                ev_o = em.op("scalar", lambda e, t=t, n=n, xs=xs, base=base: e.dma_start(
                    out=y_d[base + t * 128: base + (t + 1) * 128, n * 512:(n + 1) * 512],
                    in_=XPS[:, xs * 512:(xs + 1) * 512]), inc="od", dma=True)
                state["xps_cons"][xs] = ev_o
                if n == 3:
                    state["y_written"][tile * 8 + t] = ev_o

    def ffn_mlp(tile, which):
        src, scw = (XNT, CW) if which == 0 else (ATT, 1024)
        dst = ATT if which == 0 else VB
        w_dd = w1_d if which == 0 else w3_d
        if which == 0:
            em.wait("tensor", state["ln_ev"]["scalar"])
        ev_e = None
        for m in range(16):
            wsl = m % 2
            load_wa(wsl, w_dd[m])
            ev_w = state["wa"][wsl]
            for n in range(2):
                pm = (m * 2 + n) % 2
                pmb = BK_PM[pm]
                bk.write("tensor", pmb)
                em.wait("tensor", ev_w)
                ev_mm = None
                for c in range(DC):
                    ev_mm = em.op("tensor", lambda e, c=c, n=n, wsl=wsl, pm=pm, src=src, scw=scw: e.matmul(
                        PM[pm][0:128, 0:512],
                        lhsT=WA[:, wsl * 2048 + c * 128: wsl * 2048 + (c + 1) * 128],
                        rhs=src[:, c * scw + n * 512: c * scw + n * 512 + 512],
                        start=(c == 0), stop=(c == DC - 1)),
                        inc=("t" if c == DC - 1 else None))
                bk.wrote(pmb, ev_mm)
                state["wa_cons"][wsl] = ev_mm
                em.wait("scalar", ev_mm)
                ev_e = em.op("scalar", lambda e, m=m, n=n, pm=pm, dst=dst: e.activation(
                    out=dst[:, m * 1024 + n * 512: m * 1024 + (n + 1) * 512],
                    in_=PM[pm][0:128, 0:512], func=ACTF.Silu, bias=ZAP[0:128, 0:1], scale=1.0),
                    inc="a")
                bk.did_read(pmb, "scalar", ev_e)
        state["mlp_ev"] = ev_e

    def ffn_out(tile):
        sc_ps = [PS[0], PS[1], PV[0], PV[1]]
        sc_bk = [BK_PS[0], BK_PS[1], BK_PV[0], BK_PV[1]]
        ev_h3_gate = state.get("h1_mm_last")
        ev_acc = None
        for n in range(4):
            sl = n % 2
            load_wb(sl, w2_d[n])
            ev_w = state["wb"][sl]
            bk.write("tensor", sc_bk[n])
            ev_s = None
            for t in range(8):
                pm = (n * 8 + t) % 2
                pmb = BK_PM[pm]
                bk.write("tensor", pmb)
                em.wait("tensor", ev_w)
                ev_mm = None
                for c in range(DC):
                    ev_mm = em.op("tensor", lambda e, c=c, t=t, sl=sl, pm=pm: e.matmul(
                        PM[pm][0:128, 0:512],
                        lhsT=VB[:, c * 1024 + t * 128: c * 1024 + (t + 1) * 128],
                        rhs=WB[:, sl * DC * 512 + c * 512: sl * DC * 512 + (c + 1) * 512],
                        start=(c == 0), stop=(c == DC - 1)),
                        inc=("t" if c == DC - 1 else None))
                bk.wrote(pmb, ev_mm)
                state["wb_cons"][sl] = ev_mm
                em.wait("scalar", ev_mm)
                em.wait("scalar", ev_h3_gate)
                ev_h3 = em.op("scalar", lambda e, t=t, n=n, pm=pm: e.activation(
                    out=XNT[:, t * 2048 + n * 512: t * 2048 + (n + 1) * 512],
                    in_=PM[pm][0:128, 0:512], func=ACTF.Copy, bias=0.0, scale=1.0),
                    inc="a")
                bk.did_read(pmb, "scalar", ev_h3)
                em.waits("tensor", [ev_h3, state.get("gates_ev")])
                ev_s = em.op("tensor", lambda e, t=t, n=n: e.matmul(
                    sc_ps[n][0:8, 0:512],
                    lhsT=GT[:, t * 8:(t + 1) * 8],
                    rhs=XNT[:, t * 2048 + n * 512: t * 2048 + (n + 1) * 512],
                    start=(t == 0), stop=(t == 7)), inc="t")
            bk.wrote(sc_bk[n], ev_s)
            em.wait("vector", ev_s)
            if tile == 0:
                ev_acc = em.op("vector", lambda e, n=n: e.tensor_copy(
                    FIN[0:8, n * 512:(n + 1) * 512], sc_ps[n][0:8, 0:512]), inc="v")
            else:
                ev_acc = em.op("vector", lambda e, n=n: e.tensor_tensor(
                    out=FIN[0:8, n * 512:(n + 1) * 512],
                    in0=FIN[0:8, n * 512:(n + 1) * 512],
                    in1=sc_ps[n][0:8, 0:512], op=ALU.add), inc="v")
            bk.did_read(sc_bk[n], "vector", ev_acc)
        state["acc_ev"] = ev_acc

    def finale():
        em.wait("gpsimd", state["acc_ev"])
        ev1 = em.op("gpsimd", lambda e: e.dma_start(
            out=cc_i[:, :], in_=FIN[0:8, 0:D]), inc="gd", dma=True)
        em.wait("gpsimd", ev1)
        ev2 = em.op("gpsimd", lambda e: e.collective_compute(
            "AllReduce", ALU.add, replica_groups=[list(range(8))],
            ins=[cc_i[:, :]], outs=[cc_o[:, :]]), inc="g")
        em.wait("gpsimd", ev2)
        ev3 = em.op("gpsimd", lambda e: e.dma_start(
            out=FIN[0:8, D:2 * D], in_=cc_o[:, :]), inc="gd", dma=True)
        em.wait("sync", ev1)
        em.wait("sync", state["y_written"][0])
        ev4 = em.op("sync", lambda e: e.dma_start(
            out=FIN[0:8, 0:D], in_=y_d[0:8, :]), inc="xd", dma=True)
        em.waits("vector", [ev3, ev4])
        em.op("vector", lambda e: e.tensor_scalar(
            out=FIN[0:8, D:2 * D], in0=FIN[0:8, D:2 * D], scalar1=CM[0:8, 0:1],
            scalar2=None, op0=ALU.mult))
        ev5 = em.op("vector", lambda e: e.tensor_tensor(
            out=FIN[0:8, D:2 * D], in0=FIN[0:8, D:2 * D], in1=FIN[0:8, 0:D],
            op=ALU.add), inc="v")
        em.wait("scalar", ev5)
        em.op("scalar", lambda e: e.dma_start(
            out=y_d[0:8, :], in_=FIN[0:8, D:2 * D]), inc="od", dma=True)
        em.op("scalar", lambda e, v=em.counts["od"]: e.wait_ge(em.sems["od"], v))

    def emit_all(nsteps):
        cnt = [0]
        def step(f):
            if nsteps is None or cnt[0] < nsteps:
                f()
            cnt[0] += 1
        for tile in (0, 1):
            step(lambda tile=tile: ln_phase(tile, is_ln2=False))
            step(lambda tile=tile: v_phase(tile))
            step(lambda tile=tile: qk_attn_phase(tile))
            step(lambda tile=tile: proj_phase(tile))
            step(lambda tile=tile: ln_phase(tile, is_ln2=True))
            def _mlp0(tile=tile):
                ffn_mlp(tile, 0)
                state["h1_mm_last"] = state["wa_cons"][1]
            step(_mlp0)
            step(lambda tile=tile: ffn_mlp(tile, 1))
            step(lambda tile=tile: ffn_out(tile))
        step(finale)
    emit_all(steps)

    TENS = {"XNT": XNT, "VB": VB, "ATT": ATT, "QKB": QKB, "GT": GT, "T1R": T1R,
            "SLV": SLV, "FIN": FIN, "STT": STT, "RPS": RPS, "XCH": XCH}
    if dbg:
        for s in ("t", "v", "a", "wd", "xd", "od", "gd", "g"):
            em.wait("scalar", (s, em.counts[s]))
        for name in dbg:
            th = TENS[name]
            shp = [th.shape[0], th.shape[1]]
            dt = th.ap().dtype
            dd = nc.dram_tensor("dbg_" + name, shp, dt, kind="ExternalOutput").ap()
            em.op("scalar", lambda e, dd=dd, th=th: e.dma_start(
                out=dd[:, :], in_=th[:, :]), inc="od", dma=True)
        em.op("scalar", lambda e, v=em.counts["od"]: e.wait_ge(em.sems["od"], v))

    with nc.Block() as block:
        @block.sync
        def _(e):
            for f in em.ops["sync"]:
                f(e)

        @block.tensor
        def _(e):
            for f in em.ops["tensor"]:
                f(e)

        @block.vector
        def _(e):
            for f in em.ops["vector"]:
                f(e)

        @block.scalar
        def _(e):
            for f in em.ops["scalar"]:
                f(e)

        @block.gpsimd
        def _(e):
            for f in em.ops["gpsimd"]:
                f(e)

    es.close()
    return nc



def _host_consts():
    mm_main = np.zeros((128, 144), np.float32)
    am_main = np.full((128, 144), NEG, np.float32)
    for r in range(128):
        for c in range(r, r + 16):
            d = abs(c - r - 8)
            mm_main[r, c] = SCALE * (WIN - d / 2.0) / WIN
            am_main[r, c] = 0.0
    mm_s0 = np.zeros((8, 16), np.float32)
    am_s0 = np.full((8, 16), NEG, np.float32)
    for r in range(8):
        for c in range(0, r + 8):
            d = abs(c - r)
            mm_s0[r, c] = SCALE * (WIN - d / 2.0) / WIN
            am_s0[r, c] = 0.0
    mm_last = np.zeros((120, 128), np.float32)
    am_last = np.full((120, 128), NEG, np.float32)
    for r in range(120):
        for c in range(r, min(r + 16, 128)):
            d = abs(c - r - 8)
            mm_last[r, c] = SCALE * (WIN - d / 2.0) / WIN
            am_last[r, c] = 0.0
    return mm_main, am_main, mm_s0, am_s0, mm_last, am_last


def _prep_weights(qkv_w, proj_w, proj_b, w1, w3, w2, wr):
    f = np.float32
    qkv_w = np.asarray(qkv_w, f)
    qk = np.ascontiguousarray(qkv_w[0:4096])
    T = qk.T
    Bq = T.reshape(16, 128, 32, 128).transpose(2, 1, 0, 3)
    order = []
    for p in range(16):
        order += [p, 16 + p]
    wqk = np.ascontiguousarray(Bq[order]).reshape(32, 128, 16 * 128)

    def nslice(w):
        T = np.asarray(w, f).T
        return np.ascontiguousarray(
            T.reshape(16, 128, 4, 512).transpose(2, 1, 0, 3)).reshape(4, 128, 16 * 512)

    wv = nslice(qkv_w[4096:6144])
    wp = nslice(proj_w)
    w2n = nslice(w2)

    def mblk(w):
        T = np.asarray(w, f).T
        return np.ascontiguousarray(
            T.reshape(16, 128, 16, 128).transpose(2, 1, 0, 3)).reshape(16, 128, 16 * 128)

    w1m = mblk(w1)
    w3m = mblk(w3)
    wrh = np.ascontiguousarray(
        np.asarray(wr, f).T.reshape(16, 128, 8).transpose(1, 0, 2)).reshape(128, 128)
    pb = np.ascontiguousarray(np.asarray(proj_b, f).reshape(1, D))
    return wqk, wv, wp, w1m, w3m, w2n, wrh, pb


def kernel(L0_attn_x, L0_attn_qkv_w, L0_attn_proj_w, L0_attn_proj_b,
           L0_ffn_w1, L0_ffn_w3, L0_ffn_w2, L0_ffn_wr):
    x = np.ascontiguousarray(np.asarray(L0_attn_x, np.float32))
    wqk, wv, wp, w1m, w3m, w2n, wrh, pb = _prep_weights(
        L0_attn_qkv_w, L0_attn_proj_w, L0_attn_proj_b, L0_ffn_w1, L0_ffn_w3,
        L0_ffn_w2, L0_ffn_wr)
    mm_main, am_main, mm_s0, am_s0, mm_last, am_last = _host_consts()
    ident = np.eye(128, dtype=np.float32)
    nc = build_nc()
    in_maps = []
    for c in range(8):
        cm = np.full((8, 1), 1.0 if c == 0 else 0.0, np.float32)
        in_maps.append({
            "x": np.ascontiguousarray(x[c]),
            "wqk": wqk, "wv": wv, "wp": wp, "w1": w1m, "w3": w3m, "w2": w2n,
            "wr": wrh, "pb": pb,
            "mm_main": mm_main, "am_main": am_main, "mm_s0": mm_s0,
            "am_s0": am_s0, "mm_last": mm_last, "am_last": am_last,
            "ident": ident, "coremask": cm,
        })
    res = run_bass_kernel_spmd(nc, in_maps, core_ids=list(range(8)))
    return np.stack([np.asarray(res.results[c]["y"]) for c in range(8)], axis=0)
